# version 06
# baseline (speedup 1.0000x reference)
import contextlib
import numpy as np
import concourse.bass as bass
import concourse.mybir as mybir
from concourse.bass_utils import run_bass_kernel_spmd

F32 = mybir.dt.float32
BF16 = mybir.dt.bfloat16
AF = mybir.ActivationFunctionType
OP = mybir.AluOpType
AX = mybir.AxisListType

B, T, D, S, H = 4, 4096, 2048, 2048, 8192
TOK = 2048
TT = 512
NT = TOK // TT
NSUB = TT // 128
DC, SC, HC = D // 128, S // 128, H // 128
NSL, SLW = 8, 256
EPS = 1e-5
R_D = 1.0 / float(D)
ENGS = ("sync", "tensor", "vector", "scalar", "gpsimd")


class Ctr:
    __slots__ = ("h", "v")

    def __init__(self, handle):
        self.h = handle
        self.v = 0

    def inc(self, n=1):
        self.v += n
        return self.v


def build_nc(debug=False):
    nc = bass.Bass("TRN2", target_bir_lowering=False)

    def din(name, shape, dt=F32):
        return nc.dram_tensor(name, shape, dt, kind="ExternalInput").ap()

    x_in = din("x", [TOK, D])
    wf_h = din("wf", [SC, 128, DC, 128])
    wzg_h = din("wzg", [SC, 128, DC, 128])
    wz_h = din("wz", [SC, 128, DC, 128])
    wga_h = din("wga", [SC, 128, DC, 128])
    wgb_h = din("wgb", [SC, 128, DC, 128])
    wout_h = din("wout", [SC, 4, 128, 512])
    w13_h = din("w13", [HC, 2, 128, DC, 128])
    w2sl_h = din("w2sl", [NSL, HC, 128, SLW])
    wfb_in = din("wfb", [128, SC])
    rms_in = din("rmsw", [1, D])
    woutb_in = din("woutb", [1, D])
    ident_in = din("ident", [128, 128])
    hadp_in = din("hadp", [128, 128])
    mask_in = din("maskbc", [128, 8])

    y_out = nc.dram_tensor("y", [TOK, D], F32, kind="ExternalOutput").ap()

    w2pp = nc.dram_tensor("w2pp", [HC, 4, 128, 512], F32).ap()
    fsp = nc.dram_tensor("fsp", [NT, 128, SC * TT], BF16).ap()
    zsp = nc.dram_tensor("zsp", [NT, 128, SC * TT], BF16).ap()
    ogsp = nc.dram_tensor("ogsp", [NT, 128, SC * TT], BF16).ap()
    cc_in = nc.dram_tensor("cc_in", [1, S], F32).ap()
    cc_out = nc.dram_tensor("cc_out", [8, S], F32).ap()

    st = contextlib.ExitStack()

    O_X, O_F, O_Z = 0, 8192, 12288
    O_Y, O_FM, O_OG, O_HH = 16384, 24576, 32768, 36864
    O_WB, O_BC, O_DU, O_MS = 40960, 49152, 51200, 52224
    ACOLS = 53200
    assert O_MS + 976 <= ACOLS
    arena = nc.alloc_sbuf_tensor("arena", [128, ACOLS], F32)
    AAP = arena.ap()

    def cv(col, cols, rows=128):
        return AAP[0:rows, col:col + cols]

    def cvb(col, cols_bf16, rows=128):
        return AAP[0:rows, col:col + cols_bf16 // 2].bitcast(BF16)

    x_tm = cv(O_X, 8192)
    f_sb = cvb(O_F, 8192)
    z_sb = cvb(O_Z, 8192)
    xn_tm = cv(O_Y, 8192)
    zg_sb = cv(O_Y, 8192)
    xn_fm = cv(O_FM, 8192)
    og_sb = cvb(O_OG, 8192)
    h_sb = cvb(O_HH, 8192)
    hidden = cv(O_X, 16384)
    arenaA = cv(O_X, 16384)
    arenaB = cv(O_Y, 16384)
    y2 = cv(O_OG, 2048)
    bc_buf = cv(O_BC, 2048)
    wb_all = cv(O_WB, 8192)
    su_tmp = cv(O_HH + 1024, 1024)
    dump = cv(O_DU, 1024)
    row_st = cv(O_HH, 2048, rows=1)

    M = O_MS
    stats = cv(M, 64)
    fcar = cv(M + 64, 16)
    hcar = cv(M + 80, 16)
    acc0 = cv(M + 96, 16)
    acc1 = cv(M + 112, 16)
    s_ident = cv(M + 128, 128)
    s_hadp = cv(M + 256, 128)
    s_mask = cv(M + 384, 8)
    s_wfb = cv(M + 392, 16)
    g_sb = cv(M + 408, 128)
    ones_row = cv(M + 536, 128, rows=1)
    fdump = cvb(M + 664, 512)

    pb = [nc.alloc_psum_tensor(f"pb{i}", [128, 512], F32) for i in range(8)]

    def mksem(name):
        return Ctr(st.enter_context(nc.semaphore(name)))

    dW = mksem("dW")
    dG = mksem("dG")
    dS = mksem("dS")
    dO = mksem("dO")
    pe = mksem("pe")
    ve = mksem("ve")
    se = mksem("se")
    cc = mksem("cc")
    ALLSEMS = (dW, dG, dS, dO, pe, ve, se, cc)

    EMIT = {e: [] for e in ENGS}

    def em(e, fn):
        EMIT[e].append(fn)

    def wait(e, sem, v):
        if v > 0:
            em(e, lambda E, s=sem, v=v: E.wait_ge(s.h, v))

    def barrier():
        vals = [(s, s.v) for s in ALLSEMS]
        for e in ENGS:
            for s, v in vals:
                wait(e, s, v)

    bank_free = [(ve, 0)] * 8

    def pe_take_bank(i):
        s, v = bank_free[i]
        wait("tensor", s, v)

    def bank_freed(i, sem, val):
        bank_free[i] = (sem, val)

    def STCOL(tsub, j):
        return stats[:, tsub * 16 + j: tsub * 16 + j + 1]

    dbg_outs = []

    def dbg(name, ap, sem, val, dt=F32):
        if not debug:
            return
        t = nc.dram_tensor("dbg_" + name, list(ap.shape), dt, kind="ExternalOutput").ap()
        wait("gpsimd", sem, val)
        em("gpsimd", lambda E, t=t, a=ap: E.dma_start(out=t, in_=a).then_inc(dG.h, 16))
        dG.inc(16)
        dbg_outs.append(name)
        barrier()

    for i, val in enumerate((EPS, R_D, 0.0, 1.0, -0.5)):
        slot = cv(M + 940 + i, 1)
        em("vector", lambda E, s=slot, v=val: E.memset(s, v))
        nc.const_aps.register(F32, val, slot)
    for src, dst in ((ident_in, s_ident), (hadp_in, s_hadp), (mask_in, s_mask),
                     (wfb_in, s_wfb)):
        em("sync", lambda E, s=src, d=dst: E.dma_start(out=d, in_=s).then_inc(dW.h, 16))
        dW.inc(16)
    v_consts = dW.v

    def broadcast_row(row_in, dst):
        em("sync", lambda E, r=row_in: E.dma_start(out=row_st, in_=r).then_inc(dW.h, 16))
        vrow = dW.inc(16)
        em("vector", lambda E: E.memset(ones_row, 1.0).then_inc(ve.h, 1))
        vones = ve.inc()
        wait("tensor", dW, vrow)
        wait("tensor", ve, vones)
        for nb in range(4):
            bk = nb % 2
            pe_take_bank(bk)
            em("tensor", lambda E, bk=bk, nb=nb: E.matmul(
                pb[bk][:, :], ones_row, row_st[0:1, nb * 512:(nb + 1) * 512],
                start=True, stop=True).then_inc(pe.h, 1))
            vpe = pe.inc()
            wait("vector", pe, vpe)
            em("vector", lambda E, bk=bk, nb=nb, d=dst: E.tensor_copy(
                d[:, nb * 512:(nb + 1) * 512], pb[bk][:, :]).then_inc(ve.h, 1))
            bank_freed(bk, ve, ve.inc())

    broadcast_row(rms_in, bc_buf)
    barrier()

    for q in range(NSL):
        dq, half = q // 2, q % 2
        for c in range(HC):
            em("sync", lambda E, q=q, c=c: E.dma_start(
                out=arenaA[:, c * SLW:(c + 1) * SLW], in_=w2sl_h[q, c]).then_inc(dW.h, 16))
            dW.inc(16)
        wait("vector", dW, dW.v)
        src, dst = arenaA, arenaB
        for stg in range(6):
            bit = 1 << stg
            for c in range(HC):
                lo, hi = c & ~bit, c | bit
                op = OP.add if (c & bit) == 0 else OP.subtract
                em("vector", lambda E, c=c, lo=lo, hi=hi, op=op, s=src, d=dst:
                   E.tensor_tensor(d[:, c * SLW:(c + 1) * SLW],
                                   s[:, lo * SLW:(lo + 1) * SLW],
                                   s[:, hi * SLW:(hi + 1) * SLW], op))
            src, dst = dst, src
        em("vector", lambda E: E.engine_nop().then_inc(ve.h, 1))
        v_bfly = ve.inc()
        wait("tensor", ve, v_bfly)
        for c in range(HC):
            bk = c % 2
            pe_take_bank(bk)
            em("tensor", lambda E, c=c, bk=bk, s=src: E.matmul(
                pb[bk][:, 0:SLW], s_hadp, s[:, c * SLW:(c + 1) * SLW],
                start=True, stop=True).then_inc(pe.h, 1))
            vpe = pe.inc()
            wait("scalar", pe, vpe)
            em("scalar", lambda E, c=c, bk=bk, d=dst: E.activation(
                d[:, c * SLW:(c + 1) * SLW], pb[bk][:, 0:SLW], AF.Copy).then_inc(se.h, 1))
            vse = se.inc()
            bank_freed(bk, se, vse)
            wait("gpsimd", se, vse)
            em("gpsimd", lambda E, c=c, dq=dq, half=half, d=dst: E.dma_start(
                out=w2pp[c, dq, :, half * SLW:(half + 1) * SLW],
                in_=d[:, c * SLW:(c + 1) * SLW]).then_inc(dS.h, 16))
            dS.inc(16)
        barrier()
    v_w2pp_done = dS.v
    dbg("w2pp0", w2pp[0], dS, v_w2pp_done)
    dbg("w2pp63", w2pp[63], dS, v_w2pp_done)

    v_xfree = ve.v
    v_xnfree = pe.v
    v_fzog_scan = (ve, 0)
    v_fzog_spill = 0
    wbuf = [wb_all[:, 0:2048], wb_all[:, 2048:4096]]
    wbuf_free = [(pe, 0), (pe, 0)]

    vjunk = cv(M + 952, 8)
    sjunk = cv(M + 960, 8)

    def vsep():
        em("vector", lambda E: E.memset(vjunk, 0.0))

    def ssep():
        em("scalar", lambda E: E.activation(sjunk, vjunk, AF.Square))

    def emit_rsqrt_batch(tsubs, ms_j, out_j):
        em("vector", lambda E: E.engine_nop().then_inc(ve.h, 1))
        vv = ve.inc()
        wait("scalar", ve, vv)
        if len(tsubs) == 1:
            ssep()
        for t in tsubs:
            em("scalar", lambda E, t=t: E.activation(STCOL(t, 3), STCOL(t, ms_j), AF.Ln))
        if len(tsubs) == 1:
            ssep()
        for i, t in enumerate(tsubs):
            if i == len(tsubs) - 1:
                em("scalar", lambda E, t=t: E.activation(
                    STCOL(t, out_j), STCOL(t, 3), AF.Exp, scale=-0.5).then_inc(se.h, 1))
            else:
                em("scalar", lambda E, t=t: E.activation(
                    STCOL(t, out_j), STCOL(t, 3), AF.Exp, scale=-0.5))
        vse = se.inc()
        wait("vector", se, vse)
        for _ in range(2):
            for step in range(4):
                for t in tsubs:
                    if step == 0:
                        em("vector", lambda E, t=t: E.tensor_mul(
                            STCOL(t, 2), STCOL(t, out_j), STCOL(t, out_j)))
                    elif step == 1:
                        em("vector", lambda E, t=t: E.tensor_mul(
                            STCOL(t, 2), STCOL(t, 2), STCOL(t, ms_j)))
                    elif step == 2:
                        em("vector", lambda E, t=t: E.tensor_scalar(
                            STCOL(t, 2), STCOL(t, 2), -0.5, 1.5, OP.mult, OP.add))
                    else:
                        em("vector", lambda E, t=t: E.tensor_mul(
                            STCOL(t, out_j), STCOL(t, out_j), STCOL(t, 2)))
                if len(tsubs) == 1:
                    vsep()
        vsep()

    def emit_sq_stats(tsub, src_ap, with_mean=False):
        em("scalar", lambda E, t=tsub, a=src_ap: E.activation(
            dump[:, 0:1024], a[:, 0:1024], AF.Square, accum_out=STCOL(t, 0)))
        if with_mean:
            em("scalar", lambda E, t=tsub, a=src_ap: E.activation(
                dump[:, 0:1024], a[:, 1024:2048], AF.Square, accum_out=STCOL(t, 1)))
            em("scalar", lambda E, t=tsub, a=src_ap: E.activation(
                dump[:, 0:1024], a[:, 0:1024], AF.Identity, accum_out=STCOL(t, 14)))
            em("scalar", lambda E, t=tsub, a=src_ap: E.activation(
                dump[:, 0:1024], a[:, 1024:2048], AF.Identity,
                accum_out=STCOL(t, 15)).then_inc(se.h, 1))
        else:
            em("scalar", lambda E, t=tsub, a=src_ap: E.activation(
                dump[:, 0:1024], a[:, 1024:2048], AF.Square,
                accum_out=STCOL(t, 1)).then_inc(se.h, 1))
        return se.inc()

    v_fcar = 0
    v_xfree_se = 0
    for k in range(NT):
        wait("gpsimd", ve, v_xfree)
        wait("gpsimd", se, v_xfree_se)
        for tsub in range(NSUB):
            em("gpsimd", lambda E, k=k, tsub=tsub: E.dma_start(
                out=x_tm[:, tsub * 2048:(tsub + 1) * 2048],
                in_=x_in[k * TT + tsub * 128:k * TT + (tsub + 1) * 128, :]).then_inc(dG.h, 16))
            dG.inc(16)
        v_xload = dG.v
        wait("scalar", dG, v_xload)
        for tsub in range(NSUB):
            vsq = emit_sq_stats(tsub, x_tm[:, tsub * 2048:(tsub + 1) * 2048])
        wait("vector", se, vsq)
        for t in range(NSUB):
            em("vector", lambda E, t=t: E.tensor_tensor(
                STCOL(t, 11), STCOL(t, 0), STCOL(t, 1), OP.add))
        for t in range(NSUB):
            em("vector", lambda E, t=t: E.tensor_scalar(
                STCOL(t, 12), STCOL(t, 11), R_D, EPS, OP.mult, OP.add))
        emit_rsqrt_batch(list(range(NSUB)), 12, 4)
        wait("vector", pe, v_xnfree)
        wait("vector", dW, v_consts)
        for tsub in range(NSUB):
            em("vector", lambda E, t=tsub: E.tensor_scalar_mul(
                xn_tm[:, t * 2048:(t + 1) * 2048],
                x_tm[:, t * 2048:(t + 1) * 2048], STCOL(t, 4)))
            em("vector", lambda E, t=tsub: E.tensor_mul(
                xn_tm[:, t * 2048:(t + 1) * 2048],
                xn_tm[:, t * 2048:(t + 1) * 2048], bc_buf))
        em("vector", lambda E: E.engine_nop().then_inc(ve.h, 1))
        v_xn = ve.inc()
        if k == 0:
            dbg("stats0", stats, ve, v_xn)
            dbg("bc0", bc_buf, ve, v_xn)
            dbg("xtm0", x_tm, ve, v_xn)
            dbg("xn_tm0", xn_tm, ve, v_xn)
        wait("tensor", ve, v_xn)
        for dc in range(DC):
            bk = 6 + dc % 2
            pe_take_bank(bk)
            for i in range(NSUB):
                if i == NSUB - 1:
                    em("tensor", lambda E, dc=dc, i=i, bk=bk: E.matmul(
                        pb[bk][:, i * 128:(i + 1) * 128],
                        xn_tm[:, i * 2048 + dc * 128: i * 2048 + (dc + 1) * 128],
                        s_ident, start=(i == 0), stop=True,
                        is_transpose=True).then_inc(pe.h, 1))
                else:
                    em("tensor", lambda E, dc=dc, i=i, bk=bk: E.matmul(
                        pb[bk][:, i * 128:(i + 1) * 128],
                        xn_tm[:, i * 2048 + dc * 128: i * 2048 + (dc + 1) * 128],
                        s_ident, start=(i == 0), stop=False, is_transpose=True))
            vpe = pe.inc()
            if dc == DC - 1:
                v_xnfree = vpe
            wait("vector", pe, vpe)
            em("vector", lambda E, dc=dc, bk=bk: E.tensor_copy(
                xn_fm[:, dc * 512:(dc + 1) * 512], pb[bk][:, :]).then_inc(ve.h, 1))
            bank_freed(bk, ve, ve.inc())
        v_xnfm = ve.v
        if k == 0:
            dbg("xnfm0", xn_fm, ve, v_xnfm)

        sve, vve = v_fzog_scan
        wait("scalar", sve, vve)
        wait("scalar", dG, v_fzog_spill)
        wait("vector", dG, v_fzog_spill)

        def proj_stream(w_src, evac):
            for sc in range(SC):
                par = sc % 2
                s, v = wbuf_free[par]
                wait("sync", s, v)
                em("sync", lambda E, sc=sc, par=par, w=w_src: E.dma_start(
                    out=wbuf[par], in_=w[sc]).then_inc(dW.h, 16))
                vw = dW.inc(16)
                bk = sc % 2
                pe_take_bank(bk)
                wait("tensor", dW, vw)
                if sc == 0:
                    wait("tensor", ve, v_xnfm)
                for dc in range(DC):
                    if dc == DC - 1:
                        em("tensor", lambda E, par=par, dc=dc, bk=bk: E.matmul(
                            pb[bk][:, :], wbuf[par][:, dc * 128:(dc + 1) * 128],
                            xn_fm[:, dc * 512:(dc + 1) * 512],
                            start=False, stop=True).then_inc(pe.h, 1))
                    else:
                        em("tensor", lambda E, par=par, dc=dc, bk=bk: E.matmul(
                            pb[bk][:, :], wbuf[par][:, dc * 128:(dc + 1) * 128],
                            xn_fm[:, dc * 512:(dc + 1) * 512],
                            start=(dc == 0), stop=False))
                vpe = pe.inc()
                wbuf_free[par] = (pe, vpe)
                evac(sc, bk, vpe)

        def evac_f(sc, bk, vpe):
            wait("scalar", pe, vpe)
            em("scalar", lambda E, sc=sc, bk=bk: E.activation(
                f_sb[:, sc * 512:(sc + 1) * 512], pb[bk][:, :], AF.Sigmoid,
                bias=s_wfb[:, sc:sc + 1], scale=1.0).then_inc(se.h, 1))
            bank_freed(bk, se, se.inc())

        def evac_zg(sc, bk, vpe):
            wait("scalar", pe, vpe)
            em("scalar", lambda E, sc=sc, bk=bk: E.activation(
                zg_sb[:, sc * 512:(sc + 1) * 512], pb[bk][:, :], AF.Sigmoid).then_inc(se.h, 1))
            bank_freed(bk, se, se.inc())

        def evac_z(sc, bk, vpe):
            wait("vector", pe, vpe)
            s2, v2 = bank_free[bk]
            wait("vector", se, zg_done[sc])
            em("vector", lambda E, sc=sc, bk=bk: E.tensor_tensor(
                z_sb[:, sc * 512:(sc + 1) * 512], pb[bk][:, :],
                zg_sb[:, sc * 512:(sc + 1) * 512], OP.mult).then_inc(ve.h, 1))
            bank_freed(bk, ve, ve.inc())

        proj_stream(wf_h, evac_f)
        v_f_evac = se.v
        zg_done = {}
        sc_base = se.v
        proj_stream(wzg_h, evac_zg)
        for sc in range(SC):
            zg_done[sc] = sc_base + sc + 1
        proj_stream(wz_h, evac_z)

        for sc in range(SC):
            par = sc % 2
            s, v = wbuf_free[par]
            wait("sync", s, v)
            em("sync", lambda E, sc=sc, par=par: E.dma_start(
                out=wbuf[par], in_=wga_h[sc]).then_inc(dW.h, 16))
            vw = dW.inc(16)
            bka = 2 + sc % 2
            pe_take_bank(bka)
            wait("tensor", dW, vw)
            for dc in range(DC):
                if dc == DC - 1:
                    em("tensor", lambda E, par=par, dc=dc, bka=bka: E.matmul(
                        pb[bka][:, :], wbuf[par][:, dc * 128:(dc + 1) * 128],
                        xn_fm[:, dc * 512:(dc + 1) * 512],
                        start=False, stop=True).then_inc(pe.h, 1))
                else:
                    em("tensor", lambda E, par=par, dc=dc, bka=bka: E.matmul(
                        pb[bka][:, :], wbuf[par][:, dc * 128:(dc + 1) * 128],
                        xn_fm[:, dc * 512:(dc + 1) * 512],
                        start=(dc == 0), stop=False))
            vpe = pe.inc()
            wbuf_free[par] = (pe, vpe)
            wait("scalar", pe, vpe)
            wait("scalar", ve, su_free[par])
            em("scalar", lambda E, par=par, bka=bka: E.activation(
                su_tmp[:, par * 512:(par + 1) * 512], pb[bka][:, :], AF.Silu).then_inc(se.h, 1))
            vse = se.inc()
            bank_freed(bka, se, vse)
            par2 = 1 - par
            s, v = wbuf_free[par2]
            wait("sync", s, v)
            em("sync", lambda E, sc=sc, par2=par2: E.dma_start(
                out=wbuf[par2], in_=wgb_h[sc]).then_inc(dW.h, 16))
            vw2 = dW.inc(16)
            bkb = 4 + sc % 2
            pe_take_bank(bkb)
            wait("tensor", dW, vw2)
            for dc in range(DC):
                if dc == DC - 1:
                    em("tensor", lambda E, par2=par2, dc=dc, bkb=bkb: E.matmul(
                        pb[bkb][:, :], wbuf[par2][:, dc * 128:(dc + 1) * 128],
                        xn_fm[:, dc * 512:(dc + 1) * 512],
                        start=False, stop=True).then_inc(pe.h, 1))
                else:
                    em("tensor", lambda E, par2=par2, dc=dc, bkb=bkb: E.matmul(
                        pb[bkb][:, :], wbuf[par2][:, dc * 128:(dc + 1) * 128],
                        xn_fm[:, dc * 512:(dc + 1) * 512],
                        start=(dc == 0), stop=False))
            vpe2 = pe.inc()
            wbuf_free[par2] = (pe, vpe2)
            wait("vector", pe, vpe2)
            wait("vector", se, vse)
            em("vector", lambda E, sc=sc, par=par, bkb=bkb: E.tensor_tensor(
                og_sb[:, sc * 512:(sc + 1) * 512], pb[bkb][:, :],
                su_tmp[:, par * 512:(par + 1) * 512], OP.mult).then_inc(ve.h, 1))
            vv = ve.inc()
            bank_freed(bkb, ve, vv)
            su_free[par] = vv

        wait("vector", se, v_f_evac)
        if k > 0:
            wait("vector", se, v_fcar)
        for sc in range(SC):
            xd = AAP[0:128, sc * 256:(sc + 1) * 256].bitcast(BF16)
            if k == 0:
                ini = 0.0
            else:
                ini = fcar[:, sc:sc + 1]
            if sc == SC - 1:
                em("vector", lambda E, sc=sc, xd=xd, ini=ini: E.tensor_tensor_scan(
                    xd, f_sb[:, sc * 512:(sc + 1) * 512],
                    z_sb[:, sc * 512:(sc + 1) * 512], ini,
                    OP.mult, OP.add).then_inc(ve.h, 1))
            else:
                em("vector", lambda E, sc=sc, xd=xd, ini=ini: E.tensor_tensor_scan(
                    xd, f_sb[:, sc * 512:(sc + 1) * 512],
                    z_sb[:, sc * 512:(sc + 1) * 512], ini,
                    OP.mult, OP.add))
        v_scans = ve.inc()
        wait("scalar", ve, v_scans)
        for sc in range(SC):
            xd = AAP[0:128, sc * 256:(sc + 1) * 256].bitcast(BF16)
            if sc == SC - 1:
                em("scalar", lambda E, sc=sc, xd=xd: E.copy(
                    fcar[:, sc:sc + 1], xd[:, 511:512]).then_inc(se.h, 1))
            else:
                em("scalar", lambda E, sc=sc, xd=xd: E.copy(
                    fcar[:, sc:sc + 1], xd[:, 511:512]))
        v_fcar = se.inc()
        em("vector", lambda E: E.engine_nop().then_inc(ve.h, 1))
        v_scan = ve.inc()
        if k == 0:
            dbg("f0", f_sb, ve, v_scan, BF16)
            dbg("z0", z_sb, ve, v_scan, BF16)
            dbg("og0", og_sb, ve, v_scan, BF16)
            dbg("fcar0", fcar, ve, v_scan)
            dbg("fdump0", fdump, ve, v_scan, BF16)
        if k == 1:
            dbg("f1", f_sb, ve, v_scan, BF16)
            dbg("z1", z_sb, ve, v_scan, BF16)
            dbg("fcar1", fcar, ve, v_scan)
        if k == NT - 1:
            dbg("fcar", fcar, ve, v_scan)
        wait("gpsimd", ve, v_scan)
        em("gpsimd", lambda E, k=k: E.dma_start(out=fsp[k], in_=f_sb).then_inc(dG.h, 16))
        dG.inc(16)
        em("gpsimd", lambda E, k=k: E.dma_start(out=zsp[k], in_=z_sb).then_inc(dG.h, 16))
        dG.inc(16)
        em("gpsimd", lambda E, k=k: E.dma_start(out=ogsp[k], in_=og_sb).then_inc(dG.h, 16))
        dG.inc(16)
        v_fzog_scan = (ve, v_scan)
        v_fzog_spill = dG.v
        v_xfree = v_scan
        v_xfree_se = v_fcar

    wait("gpsimd", se, v_fcar)
    em("gpsimd", lambda E: E.dma_start(
        out=cc_in.rearrange("o (p c) -> p (o c)", p=128), in_=fcar,
        allow_slow_non_contiguous=True).then_inc(dG.h, 16))
    vdma = dG.inc(16)
    wait("gpsimd", dG, vdma)
    em("gpsimd", lambda E: E.collective_compute(
        "AllGather", OP.bypass, replica_groups=[list(range(8))],
        ins=[cc_in], outs=[cc_out]).then_inc(cc.h, 1))
    vcc = cc.inc()
    wait("gpsimd", cc, vcc)
    em("gpsimd", lambda E: E.dma_start(
        out=g_sb.rearrange("p (r c) -> p r c", r=8),
        in_=cc_out.rearrange("r (p c) -> p r c", p=128),
        allow_slow_non_contiguous=True).then_inc(dG.h, 16))
    vg = dG.inc(16)
    wait("vector", dG, vg)
    em("vector", lambda E: E.tensor_scalar_mul(acc0, g_sb[:, 0:16], s_mask[:, 0:1]))
    cur, oth = acc0, acc1
    for r in range(1, 8):
        vsep()
        em("vector", lambda E, r=r, cur=cur, oth=oth: E.scalar_tensor_tensor(
            oth, g_sb[:, r * 16:(r + 1) * 16], s_mask[:, r:r + 1], cur, OP.mult, OP.add))
        cur, oth = oth, cur
    vsep()
    em("vector", lambda E, cur=cur: E.tensor_copy(hcar, cur).then_inc(ve.h, 1))
    v_hcar = ve.inc()
    dbg("hcar", hcar, ve, v_hcar)

    wait("sync", ve, ve.v)
    broadcast_row(woutb_in, bc_buf)
    barrier()

    v_xfree2 = ve.v
    v_fzh_free = ve.v
    v_fm2free = pe.v
    v_hid_free = pe.v
    v_dO_tile = dO.v
    gated = xn_fm
    x2_fm = xn_fm
    wob = [wb_all[:, 0:2048], wb_all[:, 2048:4096]]
    w13b = [wb_all[:, 0:4096], wb_all[:, 4096:8192]]
    w2b = [wb_all[:, 4096:6144], wb_all[:, 6144:8192]]
    wob_free = [(pe, 0), (pe, 0)]
    w13_free = [(pe, 0), (pe, 0)]
    w2b_free = [(pe, 0), (pe, 0)]
    v_y2_free = dO.v

    for k in range(NT):
        wait("gpsimd", ve, v_xfree2)
        wait("gpsimd", pe, v_hid_free)
        wait("gpsimd", dO, v_dO_tile)
        for tsub in range(NSUB):
            em("gpsimd", lambda E, k=k, tsub=tsub: E.dma_start(
                out=x_tm[:, tsub * 2048:(tsub + 1) * 2048],
                in_=x_in[k * TT + tsub * 128:k * TT + (tsub + 1) * 128, :]).then_inc(dG.h, 16))
            dG.inc(16)
        wait("gpsimd", ve, v_fzh_free)
        em("gpsimd", lambda E, k=k: E.dma_start(out=f_sb, in_=fsp[k]).then_inc(dG.h, 16))
        dG.inc(16)
        em("gpsimd", lambda E, k=k: E.dma_start(out=z_sb, in_=zsp[k]).then_inc(dG.h, 16))
        dG.inc(16)
        em("gpsimd", lambda E, k=k: E.dma_start(out=og_sb, in_=ogsp[k]).then_inc(dG.h, 16))
        dG.inc(16)
        v_ld = dG.v
        wait("vector", dG, v_ld)
        if k == 0:
            wait("vector", ve, v_hcar)
        wait("vector", pe, v_fm2free)
        for sc in range(SC):
            em("vector", lambda E, sc=sc: E.tensor_tensor_scan(
                h_sb[:, sc * 512:(sc + 1) * 512], f_sb[:, sc * 512:(sc + 1) * 512],
                z_sb[:, sc * 512:(sc + 1) * 512], hcar[:, sc:sc + 1], OP.mult, OP.add))
        vsep()
        for sc in range(SC):
            em("vector", lambda E, sc=sc: E.tensor_tensor(
                og_sb[:, sc * 512:(sc + 1) * 512], og_sb[:, sc * 512:(sc + 1) * 512],
                h_sb[:, sc * 512:(sc + 1) * 512], OP.mult))
        for sc in range(SC):
            em("vector", lambda E, sc=sc: E.tensor_copy(
                gated[:, sc * 512:(sc + 1) * 512], og_sb[:, sc * 512:(sc + 1) * 512]))
        for sc in range(SC):
            em("vector", lambda E, sc=sc: E.tensor_copy(
                hcar[:, sc:sc + 1], h_sb[:, sc * 512 + 511: sc * 512 + 512]))
        em("vector", lambda E: E.engine_nop().then_inc(ve.h, 1))
        v_gated = ve.inc()
        if k == 0:
            dbg("h0", h_sb, ve, v_gated, BF16)
            dbg("gated0", gated, ve, v_gated)
        v_fzh_free = v_gated
        for tsub in range(NSUB):
            em("vector", lambda E, t=tsub: E.scalar_tensor_tensor(
                x_tm[:, t * 2048:(t + 1) * 2048],
                x_tm[:, t * 2048:(t + 1) * 2048], 2.0, bc_buf, OP.mult, OP.add))
        em("vector", lambda E: E.engine_nop().then_inc(ve.h, 1))
        v_xb = ve.inc()
        wait("tensor", ve, v_gated)
        for tsub in range(NSUB):
            grp = (tsub % 2) * 4
            for dq in range(4):
                pe_take_bank(grp + dq)
            for sc in range(SC):
                par = sc % 2
                s, v = wob_free[par]
                wait("sync", s, v)
                em("sync", lambda E, sc=sc, par=par: E.dma_start(
                    out=wob[par].rearrange("p (q n) -> p q n", q=4),
                    in_=wout_h[sc].rearrange("q p n -> p q n")).then_inc(dW.h, 16))
                vw = dW.inc(16)
                wait("tensor", dW, vw)
                for dq in range(4):
                    if dq == 3:
                        em("tensor", lambda E, sc=sc, par=par, dq=dq, tsub=tsub, grp=grp:
                           E.matmul(pb[grp + dq][:, :],
                                    gated[:, sc * 512 + tsub * 128: sc * 512 + (tsub + 1) * 128],
                                    wob[par][:, dq * 512:(dq + 1) * 512],
                                    start=(sc == 0), stop=(sc == SC - 1)).then_inc(pe.h, 1))
                    else:
                        em("tensor", lambda E, sc=sc, par=par, dq=dq, tsub=tsub, grp=grp:
                           E.matmul(pb[grp + dq][:, :],
                                    gated[:, sc * 512 + tsub * 128: sc * 512 + (tsub + 1) * 128],
                                    wob[par][:, dq * 512:(dq + 1) * 512],
                                    start=(sc == 0), stop=(sc == SC - 1)))
                vsc = pe.inc()
                wob_free[par] = (pe, vsc)
            vpe = pe.v
            wait("vector", pe, vpe)
            wait("vector", ve, v_xb)
            for dq in range(4):
                em("vector", lambda E, tsub=tsub, dq=dq, grp=grp: E.tensor_tensor(
                    xn_tm[:, tsub * 2048 + dq * 512: tsub * 2048 + (dq + 1) * 512],
                    pb[grp + dq][:, :],
                    x_tm[:, tsub * 2048 + dq * 512: tsub * 2048 + (dq + 1) * 512],
                    OP.add).then_inc(ve.h, 1))
                bank_freed(grp + dq, ve, ve.inc())
        em("vector", lambda E: E.engine_nop().then_inc(ve.h, 1))
        v_y = ve.inc()
        if k == 0:
            dbg("y0", xn_tm, ve, v_y)
        v_xfree2 = v_y
        wait("scalar", ve, v_y)
        for tsub in range(NSUB):
            vsq = emit_sq_stats(tsub, xn_tm[:, tsub * 2048:(tsub + 1) * 2048],
                                with_mean=True)
        wait("vector", se, vsq)
        for t in range(NSUB):
            em("vector", lambda E, t=t: E.tensor_tensor(
                STCOL(t, 5), STCOL(t, 14), STCOL(t, 15), OP.add))
        for t in range(NSUB):
            em("vector", lambda E, t=t: E.tensor_scalar_mul(STCOL(t, 6), STCOL(t, 5), R_D))
        for t in range(NSUB):
            em("vector", lambda E, t=t: E.tensor_tensor(
                STCOL(t, 11), STCOL(t, 0), STCOL(t, 1), OP.add))
        for t in range(NSUB):
            em("vector", lambda E, t=t: E.tensor_scalar_mul(STCOL(t, 7), STCOL(t, 11), R_D))
        for t in range(NSUB):
            em("vector", lambda E, t=t: E.tensor_mul(STCOL(t, 8), STCOL(t, 6), STCOL(t, 6)))
        for t in range(NSUB):
            em("vector", lambda E, t=t: E.tensor_tensor(
                STCOL(t, 9), STCOL(t, 7), STCOL(t, 8), OP.subtract))
        for t in range(NSUB):
            em("vector", lambda E, t=t: E.tensor_scalar_add(STCOL(t, 9), STCOL(t, 9), EPS))
        emit_rsqrt_batch(list(range(NSUB)), 9, 10)
        for t in range(NSUB):
            em("vector", lambda E, t=t: E.tensor_scalar(
                xn_tm[:, t * 2048:(t + 1) * 2048], xn_tm[:, t * 2048:(t + 1) * 2048],
                STCOL(t, 6), STCOL(t, 10), OP.subtract, OP.mult))
        em("vector", lambda E: E.engine_nop().then_inc(ve.h, 1))
        v_x2 = ve.inc()
        if k == 0:
            dbg("x2_0", xn_tm, ve, v_x2)
        wait("tensor", ve, v_x2)
        for dc in range(DC):
            bk = 6 + dc % 2
            pe_take_bank(bk)
            for i in range(NSUB):
                if i == NSUB - 1:
                    em("tensor", lambda E, dc=dc, i=i, bk=bk: E.matmul(
                        pb[bk][:, i * 128:(i + 1) * 128],
                        xn_tm[:, i * 2048 + dc * 128: i * 2048 + (dc + 1) * 128],
                        s_ident, start=(i == 0), stop=True,
                        is_transpose=True).then_inc(pe.h, 1))
                else:
                    em("tensor", lambda E, dc=dc, i=i, bk=bk: E.matmul(
                        pb[bk][:, i * 128:(i + 1) * 128],
                        xn_tm[:, i * 2048 + dc * 128: i * 2048 + (dc + 1) * 128],
                        s_ident, start=(i == 0), stop=False, is_transpose=True))
            vpe = pe.inc()
            wait("vector", pe, vpe)
            em("vector", lambda E, dc=dc, bk=bk: E.tensor_copy(
                x2_fm[:, dc * 512:(dc + 1) * 512], pb[bk][:, :]).then_inc(ve.h, 1))
            bank_freed(bk, ve, ve.inc())
        v_x2fm = ve.v
        for ffs in range(2):
            wait("vector", pe, v_hid_free)
            for hc in range(HC):
                par = hc % 2
                s, v = w13_free[par]
                wait("sync", s, v)
                em("sync", lambda E, hc=hc, par=par: E.dma_start(
                    out=w13b[par].rearrange("p (w n) -> p w n", w=2),
                    in_=w13_h[hc].rearrange("w p c q -> p w (c q)")).then_inc(dW.h, 16))
                vw = dW.inc(16)
                bu = (hc % 2) * 2
                bl = bu + 1
                pe_take_bank(bu)
                pe_take_bank(bl)
                wait("tensor", dW, vw)
                if hc == 0 and ffs == 0:
                    wait("tensor", ve, v_x2fm)
                for dc in range(DC):
                    if dc == DC - 1:
                        em("tensor", lambda E, par=par, dc=dc, bu=bu, ffs=ffs: E.matmul(
                            pb[bu][:, 0:256], w13b[par][:, dc * 128:(dc + 1) * 128],
                            x2_fm[:, dc * 512 + ffs * 256: dc * 512 + ffs * 256 + 256],
                            start=False, stop=True).then_inc(pe.h, 1))
                    else:
                        em("tensor", lambda E, par=par, dc=dc, bu=bu, ffs=ffs: E.matmul(
                            pb[bu][:, 0:256], w13b[par][:, dc * 128:(dc + 1) * 128],
                            x2_fm[:, dc * 512 + ffs * 256: dc * 512 + ffs * 256 + 256],
                            start=(dc == 0), stop=False))
                vpeU = pe.inc()
                for dc in range(DC):
                    if dc == DC - 1:
                        em("tensor", lambda E, par=par, dc=dc, bl=bl, ffs=ffs: E.matmul(
                            pb[bl][:, 0:256], w13b[par][:, 2048 + dc * 128: 2048 + (dc + 1) * 128],
                            x2_fm[:, dc * 512 + ffs * 256: dc * 512 + ffs * 256 + 256],
                            start=False, stop=True).then_inc(pe.h, 1))
                    else:
                        em("tensor", lambda E, par=par, dc=dc, bl=bl, ffs=ffs: E.matmul(
                            pb[bl][:, 0:256], w13b[par][:, 2048 + dc * 128: 2048 + (dc + 1) * 128],
                            x2_fm[:, dc * 512 + ffs * 256: dc * 512 + ffs * 256 + 256],
                            start=(dc == 0), stop=False))
                vpeL = pe.inc()
                w13_free[par] = (pe, vpeL)
                wait("scalar", pe, vpeU)
                wait("scalar", ve, su2_free[par])
                em("scalar", lambda E, hc=hc, bu=bu, par=par: E.activation(
                    su_tmp[:, par * 512:par * 512 + 256], pb[bu][:, 0:256],
                    AF.Silu).then_inc(se.h, 1))
                vse = se.inc()
                bank_freed(bu, se, vse)
                wait("vector", pe, vpeL)
                wait("vector", se, vse)
                em("vector", lambda E, hc=hc, bl=bl, par=par: E.tensor_tensor(
                    hidden[:, hc * 256:(hc + 1) * 256], pb[bl][:, 0:256],
                    su_tmp[:, par * 512:par * 512 + 256], OP.mult).then_inc(ve.h, 1))
                vv = ve.inc()
                bank_freed(bl, ve, vv)
                su2_free[par] = vv
            v_hidden = ve.v
            if k == 0 and ffs == 0:
                dbg("hidden0", hidden, ve, v_hidden)
            if k == 0 and ffs == 1:
                dbg("hidden1", hidden, ve, v_hidden)
            v_w13_done = pe.v
            barrier()
            wait("tensor", ve, v_hidden)
            if k == 0 and ffs == 0:
                wait("sync", dS, v_w2pp_done)
            for i in range(8):
                pe_take_bank(i)
            for hc in range(HC):
                par = hc % 2
                if hc < 2:
                    wait("sync", pe, v_w13_done)
                else:
                    s, v = w2b_free[par]
                    wait("sync", s, v)
                em("sync", lambda E, hc=hc, par=par: E.dma_start(
                    out=w2b[par].rearrange("p (q n) -> p q n", q=4),
                    in_=w2pp[hc].rearrange("q p n -> p q n")).then_inc(dW.h, 16))
                vw = dW.inc(16)
                wait("tensor", dW, vw)
                for ts2 in range(2):
                    for dq in range(4):
                        if ts2 == 1 and dq == 3:
                            em("tensor", lambda E, hc=hc, par=par, ts2=ts2, dq=dq: E.matmul(
                                pb[ts2 * 4 + dq][:, :],
                                hidden[:, hc * 256 + ts2 * 128: hc * 256 + ts2 * 128 + 128],
                                w2b[par][:, dq * 512:(dq + 1) * 512],
                                start=(hc == 0), stop=(hc == HC - 1)).then_inc(pe.h, 1))
                        else:
                            em("tensor", lambda E, hc=hc, par=par, ts2=ts2, dq=dq: E.matmul(
                                pb[ts2 * 4 + dq][:, :],
                                hidden[:, hc * 256 + ts2 * 128: hc * 256 + ts2 * 128 + 128],
                                w2b[par][:, dq * 512:(dq + 1) * 512],
                                start=(hc == 0), stop=(hc == HC - 1)))
                vhc = pe.inc()
                w2b_free[par] = (pe, vhc)
            vpe = pe.v
            v_hid_free = vpe
            w13_free = [(pe, vpe), (pe, vpe)]
            wait("vector", pe, vpe)
            for ts2 in range(2):
                tsub = ffs * 2 + ts2
                wait("vector", dO, v_y2_free)
                for dq in range(4):
                    em("vector", lambda E, ts2=ts2, dq=dq, tsub=tsub: E.tensor_tensor(
                        y2[:, dq * 512:(dq + 1) * 512], pb[ts2 * 4 + dq][:, :],
                        xn_tm[:, tsub * 2048 + dq * 512: tsub * 2048 + (dq + 1) * 512],
                        OP.add).then_inc(ve.h, 1))
                    bank_freed(ts2 * 4 + dq, ve, ve.inc())
                em("vector", lambda E: E.engine_nop().then_inc(ve.h, 1))
                vv0 = ve.inc()
                if k == 0 and ffs == 0 and ts2 == 0:
                    dbg("y2pre", y2, ve, vv0)
                if k == 0 and ffs == 1 and ts2 == 1:
                    dbg("y2pre11", y2, ve, vv0)
                wait("scalar", ve, vv0)
                vsq = emit_sq_stats(tsub, y2, with_mean=True)
                wait("vector", se, vsq)
                em("vector", lambda E, t=tsub: E.tensor_tensor(
                    STCOL(t, 5), STCOL(t, 14), STCOL(t, 15), OP.add))
                vsep()
                em("vector", lambda E, t=tsub: E.tensor_scalar_mul(STCOL(t, 6), STCOL(t, 5), R_D))
                em("vector", lambda E, t=tsub: E.tensor_tensor(
                    STCOL(t, 11), STCOL(t, 0), STCOL(t, 1), OP.add))
                vsep()
                em("vector", lambda E, t=tsub: E.tensor_scalar_mul(STCOL(t, 7), STCOL(t, 11), R_D))
                em("vector", lambda E, t=tsub: E.tensor_mul(STCOL(t, 8), STCOL(t, 6), STCOL(t, 6)))
                vsep()
                em("vector", lambda E, t=tsub: E.tensor_tensor(
                    STCOL(t, 9), STCOL(t, 7), STCOL(t, 8), OP.subtract))
                vsep()
                em("vector", lambda E, t=tsub: E.tensor_scalar_add(STCOL(t, 9), STCOL(t, 9), EPS))
                emit_rsqrt_batch([tsub], 9, 10)
                em("vector", lambda E, t=tsub: E.tensor_scalar(
                    y2, y2, STCOL(t, 6), STCOL(t, 10), OP.subtract, OP.mult).then_inc(ve.h, 1))
                vfin = ve.inc()
                wait("gpsimd", ve, vfin)
                em("gpsimd", lambda E, k=k, tsub=tsub: E.dma_start(
                    out=y_out[k * TT + tsub * 128: k * TT + (tsub + 1) * 128, :],
                    in_=y2).then_inc(dO.h, 16))
                v_y2_free = dO.inc(16)
        v_fm2free = pe.v
        v_dO_tile = dO.v
        wob_free = [(pe, v_fm2free), (pe, v_fm2free)]

    wait("gpsimd", dO, dO.v)

    with nc.Block() as block:
        @block.sync
        def _(E):
            for fn in EMIT["sync"]:
                fn(E)

        @block.tensor
        def _(E):
            for fn in EMIT["tensor"]:
                fn(E)

        @block.vector
        def _(E):
            for fn in EMIT["vector"]:
                fn(E)

        @block.scalar
        def _(E):
            for fn in EMIT["scalar"]:
                fn(E)

        @block.gpsimd
        def _(E):
            for fn in EMIT["gpsimd"]:
                fn(E)

    st.close()
    return nc


su_free = [0, 0]
su2_free = [0, 0]


def _host_layouts(inputs):
    f32 = np.float32

    def proj_tiles(W):
        return np.ascontiguousarray(
            np.asarray(W).reshape(SC, 128, DC, 128).transpose(0, 3, 2, 1).astype(f32))

    wf = proj_tiles(inputs["L0_attn_Wf_w"])
    wzg = proj_tiles(inputs["L0_attn_Wzg_w"])
    wz = proj_tiles(inputs["L0_attn_Wz_w"])
    wga = proj_tiles(inputs["L0_attn_Wga_w"])
    wgb = proj_tiles(inputs["L0_attn_Wgb_w"])
    Wout = np.asarray(inputs["L0_attn_Wout_w"])
    wout = np.ascontiguousarray(
        Wout.reshape(4, 512, SC, 128).transpose(2, 0, 3, 1).astype(f32))
    W1 = np.asarray(inputs["L0_ffn_W1"])
    W3 = np.asarray(inputs["L0_ffn_W3"])
    w1 = W1.reshape(HC, 128, DC, 128).transpose(0, 3, 2, 1)
    w3 = W3.reshape(HC, 128, DC, 128).transpose(0, 3, 2, 1)
    w13 = np.ascontiguousarray(np.stack([w1, w3], axis=1).astype(f32))
    W2T = np.ascontiguousarray(np.asarray(inputs["L0_ffn_W2"]).T)
    w2sl = np.ascontiguousarray(
        W2T.reshape(HC, 128, NSL, SLW).transpose(2, 0, 1, 3).astype(f32))
    wfb = np.ascontiguousarray(
        np.asarray(inputs["L0_attn_Wf_b"]).reshape(SC, 128).T.astype(f32))
    rmsw = np.asarray(inputs["L0_attn_rms_w"]).reshape(1, D).astype(f32)
    woutb = np.asarray(inputs["L0_attn_Wout_b"]).reshape(1, D).astype(f32)
    ident = np.eye(128, dtype=f32)
    idx = np.arange(128)
    pc = idx[:, None] & idx[None, :]
    pop = np.zeros_like(pc)
    v = pc.copy()
    while v.any():
        pop += v & 1
        v >>= 1
    hadp = (((-1.0) ** pop) / np.sqrt(float(H))).astype(f32)
    return dict(wf=wf, wzg=wzg, wz=wz, wga=wga, wgb=wgb, wout=wout, w13=w13,
                w2sl=w2sl, wfb=wfb, rmsw=rmsw, woutb=woutb, ident=ident, hadp=hadp)


def kernel(**inputs):
    global su_free, su2_free
    su_free = [0, 0]
    su2_free = [0, 0]
    x = np.asarray(inputs["L0_attn_x"], dtype=np.float32)
    shared = _host_layouts(inputs)
    nc = build_nc()
    in_maps = []
    for c in range(8):
        b, hf = c // 2, c % 2
        mask = np.zeros(8, np.float32)
        if hf == 1:
            mask[c - 1] = 1.0
        m = dict(shared)
        m["x"] = np.ascontiguousarray(x[b, hf * TOK:(hf + 1) * TOK, :])
        m["maskbc"] = np.ascontiguousarray(np.tile(mask[None, :], (128, 1)))
        in_maps.append(m)
    res = run_bass_kernel_spmd(nc, in_maps, core_ids=list(range(8)))
    out = np.empty((B, T, D), np.float32)
    for c in range(8):
        b, hf = c // 2, c % 2
        out[b, hf * TOK:(hf + 1) * TOK, :] = res.results[c]["y"]
    return out


# version 08
# speedup vs baseline: 1.0022x; 1.0022x over previous
import contextlib
import numpy as np
import concourse.bass as bass
import concourse.mybir as mybir
from concourse.bass_utils import run_bass_kernel_spmd

F32 = mybir.dt.float32
BF16 = mybir.dt.bfloat16
AF = mybir.ActivationFunctionType
OP = mybir.AluOpType
AX = mybir.AxisListType

B, T, D, S, H = 4, 4096, 2048, 2048, 8192
TOK = 2048
TT = 512
NT = TOK // TT
NSUB = TT // 128
DC, SC, HC = D // 128, S // 128, H // 128
NSL, SLW = 8, 256
EPS = 1e-5
R_D = 1.0 / float(D)
ENGS = ("sync", "tensor", "vector", "scalar", "gpsimd")


class Ctr:
    __slots__ = ("h", "v")

    def __init__(self, handle):
        self.h = handle
        self.v = 0

    def inc(self, n=1):
        self.v += n
        return self.v


def build_nc(debug=False):
    nc = bass.Bass("TRN2", target_bir_lowering=False)

    def din(name, shape, dt=F32):
        return nc.dram_tensor(name, shape, dt, kind="ExternalInput").ap()

    x_in = din("x", [TOK, D])
    wf_h = din("wf", [SC, 128, DC, 128])
    wzg_h = din("wzg", [SC, 128, DC, 128])
    wz_h = din("wz", [SC, 128, DC, 128])
    wga_h = din("wga", [SC, 128, DC, 128])
    wgb_h = din("wgb", [SC, 128, DC, 128])
    wout_h = din("wout", [SC, 4, 128, 512])
    w13_h = din("w13", [HC, 2, 128, DC, 128])
    w2sl_h = din("w2sl", [NSL, HC, 128, SLW])
    wfb_in = din("wfb", [128, SC])
    rms_in = din("rmsw", [1, D])
    woutb_in = din("woutb", [1, D])
    ident_in = din("ident", [128, 128])
    hadp_in = din("hadp", [128, 128])
    mask_in = din("maskbc", [128, 8])

    y_out = nc.dram_tensor("y", [TOK, D], F32, kind="ExternalOutput").ap()

    w2pp = nc.dram_tensor("w2pp", [HC, 4, 128, 512], F32).ap()
    fsp = nc.dram_tensor("fsp", [NT, 128, SC * TT], BF16).ap()
    zsp = nc.dram_tensor("zsp", [NT, 128, SC * TT], BF16).ap()
    ogsp = nc.dram_tensor("ogsp", [NT, 128, SC * TT], BF16).ap()
    cc_in = nc.dram_tensor("cc_in", [1, S], F32).ap()
    cc_out = nc.dram_tensor("cc_out", [8, S], F32).ap()

    st = contextlib.ExitStack()

    O_X, O_F, O_Z = 0, 8192, 12288
    O_Y, O_FM, O_OG, O_HH = 16384, 24576, 32768, 36864
    O_WB, O_BC, O_DU, O_MS = 40960, 49152, 51200, 52224
    ACOLS = 53200
    assert O_MS + 976 <= ACOLS
    arena = nc.alloc_sbuf_tensor("arena", [128, ACOLS], F32)
    AAP = arena.ap()

    def cv(col, cols, rows=128):
        return AAP[0:rows, col:col + cols]

    def cvb(col, cols_bf16, rows=128):
        return AAP[0:rows, col:col + cols_bf16 // 2].bitcast(BF16)

    x_tm = cv(O_X, 8192)
    f_sb = cvb(O_F, 8192)
    z_sb = cvb(O_Z, 8192)
    xn_tm = cv(O_Y, 8192)
    zg_sb = cv(O_Y, 8192)
    xn_fm = cv(O_FM, 8192)
    og_sb = cvb(O_OG, 8192)
    h_sb = cvb(O_HH, 8192)
    hidden = cv(O_X, 16384)
    arenaA = cv(O_X, 16384)
    arenaB = cv(O_Y, 16384)
    y2 = cv(O_OG, 2048)
    y2b = cv(O_OG + 2048, 2048)
    bc_buf = cv(O_BC, 2048)
    wb_all = cv(O_WB, 8192)
    su_tmp = cv(O_HH + 1024, 1024)
    dump = cv(O_DU, 1024)
    row_st = cv(O_HH, 2048, rows=1)

    M = O_MS
    stats = cv(M, 64)
    fcar = cv(M + 64, 16)
    hcar = cv(M + 80, 16)
    acc0 = cv(M + 96, 16)
    acc1 = cv(M + 112, 16)
    s_ident = cv(M + 128, 128)
    s_hadp = cv(M + 256, 128)
    s_mask = cv(M + 384, 8)
    s_wfb = cv(M + 392, 16)
    g_sb = cv(M + 408, 128)
    ones_row = cv(M + 536, 128, rows=1)
    fdump = cvb(M + 664, 512)

    pb = [nc.alloc_psum_tensor(f"pb{i}", [128, 512], F32) for i in range(8)]

    def mksem(name):
        return Ctr(st.enter_context(nc.semaphore(name)))

    dW = mksem("dW")
    dG = mksem("dG")
    dS = mksem("dS")
    dO = mksem("dO")
    pe = mksem("pe")
    ve = mksem("ve")
    se = mksem("se")
    cc = mksem("cc")
    ALLSEMS = (dW, dG, dS, dO, pe, ve, se, cc)

    EMIT = {e: [] for e in ENGS}

    def em(e, fn):
        EMIT[e].append(fn)

    def wait(e, sem, v):
        if v > 0:
            em(e, lambda E, s=sem, v=v: E.wait_ge(s.h, v))

    def barrier():
        vals = [(s, s.v) for s in ALLSEMS]
        for e in ENGS:
            for s, v in vals:
                wait(e, s, v)

    bank_free = [(ve, 0)] * 8

    def pe_take_bank(i):
        s, v = bank_free[i]
        wait("tensor", s, v)

    def bank_freed(i, sem, val):
        bank_free[i] = (sem, val)

    def STCOL(tsub, j):
        return stats[:, tsub * 16 + j: tsub * 16 + j + 1]

    dbg_outs = []

    def dbgnb(name, ap, sem, val, dt=F32):
        if not debug:
            return
        t = nc.dram_tensor("dbg_" + name, list(ap.shape), dt, kind="ExternalOutput").ap()
        wait("gpsimd", sem, val)
        em("gpsimd", lambda E, t=t, a=ap: E.dma_start(out=t, in_=a).then_inc(dG.h, 16))
        dG.inc(16)

    def dbg(name, ap, sem, val, dt=F32):
        if not debug:
            return
        t = nc.dram_tensor("dbg_" + name, list(ap.shape), dt, kind="ExternalOutput").ap()
        wait("gpsimd", sem, val)
        em("gpsimd", lambda E, t=t, a=ap: E.dma_start(out=t, in_=a).then_inc(dG.h, 16))
        dG.inc(16)
        dbg_outs.append(name)
        barrier()

    for i, val in enumerate((EPS, R_D, 0.0, 1.0, -0.5)):
        slot = cv(M + 940 + i, 1)
        em("vector", lambda E, s=slot, v=val: E.memset(s, v))
        nc.const_aps.register(F32, val, slot)
    for src, dst in ((ident_in, s_ident), (hadp_in, s_hadp), (mask_in, s_mask),
                     (wfb_in, s_wfb)):
        em("sync", lambda E, s=src, d=dst: E.dma_start(out=d, in_=s).then_inc(dW.h, 16))
        dW.inc(16)
    v_consts = dW.v

    def broadcast_row(row_in, dst):
        em("sync", lambda E, r=row_in: E.dma_start(out=row_st, in_=r).then_inc(dW.h, 16))
        vrow = dW.inc(16)
        em("vector", lambda E: E.memset(ones_row, 1.0).then_inc(ve.h, 1))
        vones = ve.inc()
        wait("tensor", dW, vrow)
        wait("tensor", ve, vones)
        for nb in range(4):
            bk = nb % 2
            pe_take_bank(bk)
            em("tensor", lambda E, bk=bk, nb=nb: E.matmul(
                pb[bk][:, :], ones_row, row_st[0:1, nb * 512:(nb + 1) * 512],
                start=True, stop=True).then_inc(pe.h, 1))
            vpe = pe.inc()
            wait("vector", pe, vpe)
            em("vector", lambda E, bk=bk, nb=nb, d=dst: E.tensor_copy(
                d[:, nb * 512:(nb + 1) * 512], pb[bk][:, :]).then_inc(ve.h, 1))
            bank_freed(bk, ve, ve.inc())

    broadcast_row(rms_in, bc_buf)
    barrier()

    for q in range(NSL):
        dq, half = q // 2, q % 2
        for c in range(HC):
            em("sync", lambda E, q=q, c=c: E.dma_start(
                out=arenaA[:, c * SLW:(c + 1) * SLW], in_=w2sl_h[q, c]).then_inc(dW.h, 16))
            dW.inc(16)
        wait("vector", dW, dW.v)
        src, dst = arenaA, arenaB
        for stg in range(6):
            bit = 1 << stg
            for c in range(HC):
                lo, hi = c & ~bit, c | bit
                op = OP.add if (c & bit) == 0 else OP.subtract
                em("vector", lambda E, c=c, lo=lo, hi=hi, op=op, s=src, d=dst:
                   E.tensor_tensor(d[:, c * SLW:(c + 1) * SLW],
                                   s[:, lo * SLW:(lo + 1) * SLW],
                                   s[:, hi * SLW:(hi + 1) * SLW], op))
            src, dst = dst, src
        em("vector", lambda E: E.engine_nop().then_inc(ve.h, 1))
        v_bfly = ve.inc()
        wait("tensor", ve, v_bfly)
        for c in range(HC):
            bk = c % 2
            pe_take_bank(bk)
            em("tensor", lambda E, c=c, bk=bk, s=src: E.matmul(
                pb[bk][:, 0:SLW], s_hadp, s[:, c * SLW:(c + 1) * SLW],
                start=True, stop=True).then_inc(pe.h, 1))
            vpe = pe.inc()
            wait("scalar", pe, vpe)
            em("scalar", lambda E, c=c, bk=bk, d=dst: E.activation(
                d[:, c * SLW:(c + 1) * SLW], pb[bk][:, 0:SLW], AF.Copy).then_inc(se.h, 1))
            vse = se.inc()
            bank_freed(bk, se, vse)
            wait("gpsimd", se, vse)
            em("gpsimd", lambda E, c=c, dq=dq, half=half, d=dst: E.dma_start(
                out=w2pp[c, dq, :, half * SLW:(half + 1) * SLW],
                in_=d[:, c * SLW:(c + 1) * SLW]).then_inc(dS.h, 16))
            dS.inc(16)
        barrier()
    v_w2pp_done = dS.v
    dbg("w2pp0", w2pp[0], dS, v_w2pp_done)
    dbg("w2pp63", w2pp[63], dS, v_w2pp_done)

    v_xfree = ve.v
    v_xnfree = pe.v
    v_fzog_scan = (ve, 0)
    v_fzog_spill = 0
    wbuf = [wb_all[:, 0:2048], wb_all[:, 2048:4096]]
    wbuf_free = [(pe, 0), (pe, 0)]

    vjunk = cv(M + 952, 8)
    sjunk = cv(M + 960, 8)

    def vsep():
        em("vector", lambda E: E.memset(vjunk, 0.0))

    def ssep():
        em("scalar", lambda E: E.activation(sjunk, vjunk, AF.Square))

    def emit_rsqrt_batch(tsubs, ms_j, out_j):
        em("vector", lambda E: E.engine_nop().then_inc(ve.h, 1))
        vv = ve.inc()
        wait("scalar", ve, vv)
        if len(tsubs) == 1:
            ssep()
        for t in tsubs:
            em("scalar", lambda E, t=t: E.activation(STCOL(t, 3), STCOL(t, ms_j), AF.Ln))
        if len(tsubs) == 1:
            ssep()
        for i, t in enumerate(tsubs):
            if i == len(tsubs) - 1:
                em("scalar", lambda E, t=t: E.activation(
                    STCOL(t, out_j), STCOL(t, 3), AF.Exp, scale=-0.5).then_inc(se.h, 1))
            else:
                em("scalar", lambda E, t=t: E.activation(
                    STCOL(t, out_j), STCOL(t, 3), AF.Exp, scale=-0.5))
        vse = se.inc()
        wait("vector", se, vse)
        for _ in range(2):
            for step in range(4):
                for t in tsubs:
                    if step == 0:
                        em("vector", lambda E, t=t: E.tensor_mul(
                            STCOL(t, 2), STCOL(t, out_j), STCOL(t, out_j)))
                    elif step == 1:
                        em("vector", lambda E, t=t: E.tensor_mul(
                            STCOL(t, 2), STCOL(t, 2), STCOL(t, ms_j)))
                    elif step == 2:
                        em("vector", lambda E, t=t: E.tensor_scalar(
                            STCOL(t, 2), STCOL(t, 2), -0.5, 1.5, OP.mult, OP.add))
                    else:
                        em("vector", lambda E, t=t: E.tensor_mul(
                            STCOL(t, out_j), STCOL(t, out_j), STCOL(t, 2)))
                if len(tsubs) == 1:
                    vsep()
        vsep()

    def emit_sq_stats(tsub, src_ap, with_mean=False):
        em("scalar", lambda E, t=tsub, a=src_ap: E.activation(
            dump[:, 0:1024], a[:, 0:1024], AF.Square, accum_out=STCOL(t, 0)))
        if with_mean:
            em("scalar", lambda E, t=tsub, a=src_ap: E.activation(
                dump[:, 0:1024], a[:, 1024:2048], AF.Square, accum_out=STCOL(t, 1)))
            em("scalar", lambda E, t=tsub, a=src_ap: E.activation(
                dump[:, 0:1024], a[:, 0:1024], AF.Identity, accum_out=STCOL(t, 14)))
            em("scalar", lambda E, t=tsub, a=src_ap: E.activation(
                dump[:, 0:1024], a[:, 1024:2048], AF.Identity,
                accum_out=STCOL(t, 15)).then_inc(se.h, 1))
        else:
            em("scalar", lambda E, t=tsub, a=src_ap: E.activation(
                dump[:, 0:1024], a[:, 1024:2048], AF.Square,
                accum_out=STCOL(t, 1)).then_inc(se.h, 1))
        return se.inc()

    v_fcar = 0
    v_xfree_se = 0
    for k in range(NT):
        wait("gpsimd", ve, v_xfree)
        wait("gpsimd", se, v_xfree_se)
        for tsub in range(NSUB):
            em("gpsimd", lambda E, k=k, tsub=tsub: E.dma_start(
                out=x_tm[:, tsub * 2048:(tsub + 1) * 2048],
                in_=x_in[k * TT + tsub * 128:k * TT + (tsub + 1) * 128, :]).then_inc(dG.h, 16))
            dG.inc(16)
        v_xload = dG.v
        wait("scalar", dG, v_xload)
        for tsub in range(NSUB):
            vsq = emit_sq_stats(tsub, x_tm[:, tsub * 2048:(tsub + 1) * 2048])
        wait("vector", se, vsq)
        for t in range(NSUB):
            em("vector", lambda E, t=t: E.tensor_tensor(
                STCOL(t, 11), STCOL(t, 0), STCOL(t, 1), OP.add))
        for t in range(NSUB):
            em("vector", lambda E, t=t: E.tensor_scalar(
                STCOL(t, 12), STCOL(t, 11), R_D, EPS, OP.mult, OP.add))
        emit_rsqrt_batch(list(range(NSUB)), 12, 4)
        wait("vector", pe, v_xnfree)
        wait("vector", dW, v_consts)
        for tsub in range(NSUB):
            em("vector", lambda E, t=tsub: E.tensor_scalar_mul(
                xn_tm[:, t * 2048:(t + 1) * 2048],
                x_tm[:, t * 2048:(t + 1) * 2048], STCOL(t, 4)))
            em("vector", lambda E, t=tsub: E.tensor_mul(
                xn_tm[:, t * 2048:(t + 1) * 2048],
                xn_tm[:, t * 2048:(t + 1) * 2048], bc_buf))
        em("vector", lambda E: E.engine_nop().then_inc(ve.h, 1))
        v_xn = ve.inc()
        if k == 0:
            dbg("stats0", stats, ve, v_xn)
            dbg("bc0", bc_buf, ve, v_xn)
            dbg("xtm0", x_tm, ve, v_xn)
            dbg("xn_tm0", xn_tm, ve, v_xn)
        wait("tensor", ve, v_xn)
        for dc in range(DC):
            bk = 6 + dc % 2
            pe_take_bank(bk)
            for i in range(NSUB):
                if i == NSUB - 1:
                    em("tensor", lambda E, dc=dc, i=i, bk=bk: E.matmul(
                        pb[bk][:, i * 128:(i + 1) * 128],
                        xn_tm[:, i * 2048 + dc * 128: i * 2048 + (dc + 1) * 128],
                        s_ident, start=(i == 0), stop=True,
                        is_transpose=True).then_inc(pe.h, 1))
                else:
                    em("tensor", lambda E, dc=dc, i=i, bk=bk: E.matmul(
                        pb[bk][:, i * 128:(i + 1) * 128],
                        xn_tm[:, i * 2048 + dc * 128: i * 2048 + (dc + 1) * 128],
                        s_ident, start=(i == 0), stop=False, is_transpose=True))
            vpe = pe.inc()
            if dc == DC - 1:
                v_xnfree = vpe
            wait("vector", pe, vpe)
            em("vector", lambda E, dc=dc, bk=bk: E.tensor_copy(
                xn_fm[:, dc * 512:(dc + 1) * 512], pb[bk][:, :]).then_inc(ve.h, 1))
            bank_freed(bk, ve, ve.inc())
        v_xnfm = ve.v
        if k == 0:
            dbg("xnfm0", xn_fm, ve, v_xnfm)

        sve, vve = v_fzog_scan
        wait("scalar", sve, vve)
        wait("scalar", dG, v_fzog_spill)
        wait("vector", dG, v_fzog_spill)

        def proj_stream(w_src, evac):
            for sc in range(SC):
                par = sc % 2
                s, v = wbuf_free[par]
                wait("sync", s, v)
                em("sync", lambda E, sc=sc, par=par, w=w_src: E.dma_start(
                    out=wbuf[par], in_=w[sc]).then_inc(dW.h, 16))
                vw = dW.inc(16)
                bk = sc % 2
                pe_take_bank(bk)
                wait("tensor", dW, vw)
                if sc == 0:
                    wait("tensor", ve, v_xnfm)
                for dc in range(DC):
                    if dc == DC - 1:
                        em("tensor", lambda E, par=par, dc=dc, bk=bk: E.matmul(
                            pb[bk][:, :], wbuf[par][:, dc * 128:(dc + 1) * 128],
                            xn_fm[:, dc * 512:(dc + 1) * 512],
                            start=False, stop=True).then_inc(pe.h, 1))
                    else:
                        em("tensor", lambda E, par=par, dc=dc, bk=bk: E.matmul(
                            pb[bk][:, :], wbuf[par][:, dc * 128:(dc + 1) * 128],
                            xn_fm[:, dc * 512:(dc + 1) * 512],
                            start=(dc == 0), stop=False))
                vpe = pe.inc()
                wbuf_free[par] = (pe, vpe)
                evac(sc, bk, vpe)

        def evac_f(sc, bk, vpe):
            wait("scalar", pe, vpe)
            em("scalar", lambda E, sc=sc, bk=bk: E.activation(
                f_sb[:, sc * 512:(sc + 1) * 512], pb[bk][:, :], AF.Sigmoid,
                bias=s_wfb[:, sc:sc + 1], scale=1.0).then_inc(se.h, 1))
            bank_freed(bk, se, se.inc())

        def evac_zg(sc, bk, vpe):
            wait("scalar", pe, vpe)
            em("scalar", lambda E, sc=sc, bk=bk: E.activation(
                zg_sb[:, sc * 512:(sc + 1) * 512], pb[bk][:, :], AF.Sigmoid).then_inc(se.h, 1))
            bank_freed(bk, se, se.inc())

        def evac_z(sc, bk, vpe):
            wait("vector", pe, vpe)
            s2, v2 = bank_free[bk]
            wait("vector", se, zg_done[sc])
            em("vector", lambda E, sc=sc, bk=bk: E.tensor_tensor(
                z_sb[:, sc * 512:(sc + 1) * 512], pb[bk][:, :],
                zg_sb[:, sc * 512:(sc + 1) * 512], OP.mult).then_inc(ve.h, 1))
            bank_freed(bk, ve, ve.inc())

        proj_stream(wf_h, evac_f)
        v_f_evac = se.v
        zg_done = {}
        sc_base = se.v
        proj_stream(wzg_h, evac_zg)
        for sc in range(SC):
            zg_done[sc] = sc_base + sc + 1
        proj_stream(wz_h, evac_z)

        for sc in range(SC):
            par = sc % 2
            s, v = wbuf_free[par]
            wait("sync", s, v)
            em("sync", lambda E, sc=sc, par=par: E.dma_start(
                out=wbuf[par], in_=wga_h[sc]).then_inc(dW.h, 16))
            vw = dW.inc(16)
            bka = 2 + sc % 2
            pe_take_bank(bka)
            wait("tensor", dW, vw)
            for dc in range(DC):
                if dc == DC - 1:
                    em("tensor", lambda E, par=par, dc=dc, bka=bka: E.matmul(
                        pb[bka][:, :], wbuf[par][:, dc * 128:(dc + 1) * 128],
                        xn_fm[:, dc * 512:(dc + 1) * 512],
                        start=False, stop=True).then_inc(pe.h, 1))
                else:
                    em("tensor", lambda E, par=par, dc=dc, bka=bka: E.matmul(
                        pb[bka][:, :], wbuf[par][:, dc * 128:(dc + 1) * 128],
                        xn_fm[:, dc * 512:(dc + 1) * 512],
                        start=(dc == 0), stop=False))
            vpe = pe.inc()
            wbuf_free[par] = (pe, vpe)
            wait("scalar", pe, vpe)
            wait("scalar", ve, su_free[par])
            em("scalar", lambda E, par=par, bka=bka: E.activation(
                su_tmp[:, par * 512:(par + 1) * 512], pb[bka][:, :], AF.Silu).then_inc(se.h, 1))
            vse = se.inc()
            bank_freed(bka, se, vse)
            par2 = 1 - par
            s, v = wbuf_free[par2]
            wait("sync", s, v)
            em("sync", lambda E, sc=sc, par2=par2: E.dma_start(
                out=wbuf[par2], in_=wgb_h[sc]).then_inc(dW.h, 16))
            vw2 = dW.inc(16)
            bkb = 4 + sc % 2
            pe_take_bank(bkb)
            wait("tensor", dW, vw2)
            for dc in range(DC):
                if dc == DC - 1:
                    em("tensor", lambda E, par2=par2, dc=dc, bkb=bkb: E.matmul(
                        pb[bkb][:, :], wbuf[par2][:, dc * 128:(dc + 1) * 128],
                        xn_fm[:, dc * 512:(dc + 1) * 512],
                        start=False, stop=True).then_inc(pe.h, 1))
                else:
                    em("tensor", lambda E, par2=par2, dc=dc, bkb=bkb: E.matmul(
                        pb[bkb][:, :], wbuf[par2][:, dc * 128:(dc + 1) * 128],
                        xn_fm[:, dc * 512:(dc + 1) * 512],
                        start=(dc == 0), stop=False))
            vpe2 = pe.inc()
            wbuf_free[par2] = (pe, vpe2)
            wait("vector", pe, vpe2)
            wait("vector", se, vse)
            em("vector", lambda E, sc=sc, par=par, bkb=bkb: E.tensor_tensor(
                og_sb[:, sc * 512:(sc + 1) * 512], pb[bkb][:, :],
                su_tmp[:, par * 512:(par + 1) * 512], OP.mult).then_inc(ve.h, 1))
            vv = ve.inc()
            bank_freed(bkb, ve, vv)
            su_free[par] = vv

        wait("vector", se, v_f_evac)
        if k > 0:
            wait("vector", se, v_fcar)
        for sc in range(SC):
            xd = AAP[0:128, sc * 256:(sc + 1) * 256].bitcast(BF16)
            if k == 0:
                ini = 0.0
            else:
                ini = fcar[:, sc:sc + 1]
            if sc == SC - 1:
                em("vector", lambda E, sc=sc, xd=xd, ini=ini: E.tensor_tensor_scan(
                    xd, f_sb[:, sc * 512:(sc + 1) * 512],
                    z_sb[:, sc * 512:(sc + 1) * 512], ini,
                    OP.mult, OP.add).then_inc(ve.h, 1))
            else:
                em("vector", lambda E, sc=sc, xd=xd, ini=ini: E.tensor_tensor_scan(
                    xd, f_sb[:, sc * 512:(sc + 1) * 512],
                    z_sb[:, sc * 512:(sc + 1) * 512], ini,
                    OP.mult, OP.add))
        v_scans = ve.inc()
        wait("scalar", ve, v_scans)
        for sc in range(SC):
            xd = AAP[0:128, sc * 256:(sc + 1) * 256].bitcast(BF16)
            if sc == SC - 1:
                em("scalar", lambda E, sc=sc, xd=xd: E.copy(
                    fcar[:, sc:sc + 1], xd[:, 511:512]).then_inc(se.h, 1))
            else:
                em("scalar", lambda E, sc=sc, xd=xd: E.copy(
                    fcar[:, sc:sc + 1], xd[:, 511:512]))
        v_fcar = se.inc()
        em("vector", lambda E: E.engine_nop().then_inc(ve.h, 1))
        v_scan = ve.inc()
        if k == 0:
            dbg("f0", f_sb, ve, v_scan, BF16)
            dbg("z0", z_sb, ve, v_scan, BF16)
            dbg("og0", og_sb, ve, v_scan, BF16)
            dbg("fcar0", fcar, ve, v_scan)
            dbg("fdump0", fdump, ve, v_scan, BF16)
        if k == 1:
            dbg("f1", f_sb, ve, v_scan, BF16)
            dbg("z1", z_sb, ve, v_scan, BF16)
            dbg("fcar1", fcar, ve, v_scan)
        if k == NT - 1:
            dbg("fcar", fcar, ve, v_scan)
        wait("gpsimd", ve, v_scan)
        em("gpsimd", lambda E, k=k: E.dma_start(out=fsp[k], in_=f_sb).then_inc(dG.h, 16))
        dG.inc(16)
        em("gpsimd", lambda E, k=k: E.dma_start(out=zsp[k], in_=z_sb).then_inc(dG.h, 16))
        dG.inc(16)
        em("gpsimd", lambda E, k=k: E.dma_start(out=ogsp[k], in_=og_sb).then_inc(dG.h, 16))
        dG.inc(16)
        v_fzog_scan = (ve, v_scan)
        v_fzog_spill = dG.v
        v_xfree = v_scan
        v_xfree_se = v_fcar

    wait("gpsimd", se, v_fcar)
    em("gpsimd", lambda E: E.dma_start(
        out=cc_in.rearrange("o (p c) -> p (o c)", p=128), in_=fcar,
        allow_slow_non_contiguous=True).then_inc(dG.h, 16))
    vdma = dG.inc(16)
    wait("gpsimd", dG, vdma)
    em("gpsimd", lambda E: E.collective_compute(
        "AllGather", OP.bypass, replica_groups=[list(range(8))],
        ins=[cc_in], outs=[cc_out]).then_inc(cc.h, 1))
    vcc = cc.inc()
    wait("gpsimd", cc, vcc)
    em("gpsimd", lambda E: E.dma_start(
        out=g_sb.rearrange("p (r c) -> p r c", r=8),
        in_=cc_out.rearrange("r (p c) -> p r c", p=128),
        allow_slow_non_contiguous=True).then_inc(dG.h, 16))
    vg = dG.inc(16)
    wait("vector", dG, vg)
    em("vector", lambda E: E.tensor_scalar_mul(acc0, g_sb[:, 0:16], s_mask[:, 0:1]))
    cur, oth = acc0, acc1
    for r in range(1, 8):
        vsep()
        em("vector", lambda E, r=r, cur=cur, oth=oth: E.scalar_tensor_tensor(
            oth, g_sb[:, r * 16:(r + 1) * 16], s_mask[:, r:r + 1], cur, OP.mult, OP.add))
        cur, oth = oth, cur
    vsep()
    em("vector", lambda E, cur=cur: E.tensor_copy(hcar, cur).then_inc(ve.h, 1))
    v_hcar = ve.inc()
    dbg("hcar", hcar, ve, v_hcar)

    wait("sync", ve, ve.v)
    broadcast_row(woutb_in, bc_buf)
    barrier()

    v_xfree2 = ve.v
    v_fzh_free = ve.v
    v_fm2free = pe.v
    v_hid_free = pe.v
    v_dO_tile = dO.v
    gated = xn_fm
    x2_fm = xn_fm
    wob = [wb_all[:, 0:2048], wb_all[:, 2048:4096]]
    w13b = [wb_all[:, 0:4096], wb_all[:, 4096:8192]]
    w2b = [wb_all[:, 4096:6144], wb_all[:, 6144:8192]]
    wob_free = [(pe, 0), (pe, 0)]
    w13_free = [(pe, 0), (pe, 0)]
    w2b_free = [(pe, 0), (pe, 0)]
    v_y2_free = dO.v

    for k in range(NT):
        wait("gpsimd", ve, v_xfree2)
        wait("gpsimd", pe, v_hid_free)
        wait("gpsimd", dO, v_dO_tile)
        for tsub in range(NSUB):
            em("gpsimd", lambda E, k=k, tsub=tsub: E.dma_start(
                out=x_tm[:, tsub * 2048:(tsub + 1) * 2048],
                in_=x_in[k * TT + tsub * 128:k * TT + (tsub + 1) * 128, :]).then_inc(dG.h, 16))
            dG.inc(16)
        wait("gpsimd", ve, v_fzh_free)
        em("gpsimd", lambda E, k=k: E.dma_start(out=f_sb, in_=fsp[k]).then_inc(dG.h, 16))
        dG.inc(16)
        em("gpsimd", lambda E, k=k: E.dma_start(out=z_sb, in_=zsp[k]).then_inc(dG.h, 16))
        dG.inc(16)
        em("gpsimd", lambda E, k=k: E.dma_start(out=og_sb, in_=ogsp[k]).then_inc(dG.h, 16))
        dG.inc(16)
        v_ld = dG.v
        wait("vector", dG, v_ld)
        if k == 0:
            wait("vector", ve, v_hcar)
        wait("vector", pe, v_fm2free)
        for sc in range(SC):
            em("vector", lambda E, sc=sc: E.tensor_tensor_scan(
                h_sb[:, sc * 512:(sc + 1) * 512], f_sb[:, sc * 512:(sc + 1) * 512],
                z_sb[:, sc * 512:(sc + 1) * 512], hcar[:, sc:sc + 1], OP.mult, OP.add))
        vsep()
        for sc in range(SC):
            em("vector", lambda E, sc=sc: E.tensor_tensor(
                og_sb[:, sc * 512:(sc + 1) * 512], og_sb[:, sc * 512:(sc + 1) * 512],
                h_sb[:, sc * 512:(sc + 1) * 512], OP.mult))
        for sc in range(SC):
            em("vector", lambda E, sc=sc: E.tensor_copy(
                gated[:, sc * 512:(sc + 1) * 512], og_sb[:, sc * 512:(sc + 1) * 512]))
        for sc in range(SC):
            em("vector", lambda E, sc=sc: E.tensor_copy(
                hcar[:, sc:sc + 1], h_sb[:, sc * 512 + 511: sc * 512 + 512]))
        em("vector", lambda E: E.engine_nop().then_inc(ve.h, 1))
        v_gated = ve.inc()
        if k == 0:
            dbg("h0", h_sb, ve, v_gated, BF16)
            dbg("gated0", gated, ve, v_gated)
        v_fzh_free = v_gated
        for tsub in range(NSUB):
            em("vector", lambda E, t=tsub: E.scalar_tensor_tensor(
                x_tm[:, t * 2048:(t + 1) * 2048],
                x_tm[:, t * 2048:(t + 1) * 2048], 2.0, bc_buf, OP.mult, OP.add))
        em("vector", lambda E: E.engine_nop().then_inc(ve.h, 1))
        v_xb = ve.inc()
        wait("tensor", ve, v_gated)
        for tsub in range(NSUB):
            grp = (tsub % 2) * 4
            for dq in range(4):
                pe_take_bank(grp + dq)
            for sc in range(SC):
                par = sc % 2
                s, v = wob_free[par]
                wait("sync", s, v)
                em("sync", lambda E, sc=sc, par=par: E.dma_start(
                    out=wob[par].rearrange("p (q n) -> p q n", q=4),
                    in_=wout_h[sc].rearrange("q p n -> p q n")).then_inc(dW.h, 16))
                vw = dW.inc(16)
                wait("tensor", dW, vw)
                for dq in range(4):
                    if dq == 3:
                        em("tensor", lambda E, sc=sc, par=par, dq=dq, tsub=tsub, grp=grp:
                           E.matmul(pb[grp + dq][:, :],
                                    gated[:, sc * 512 + tsub * 128: sc * 512 + (tsub + 1) * 128],
                                    wob[par][:, dq * 512:(dq + 1) * 512],
                                    start=(sc == 0), stop=(sc == SC - 1)).then_inc(pe.h, 1))
                    else:
                        em("tensor", lambda E, sc=sc, par=par, dq=dq, tsub=tsub, grp=grp:
                           E.matmul(pb[grp + dq][:, :],
                                    gated[:, sc * 512 + tsub * 128: sc * 512 + (tsub + 1) * 128],
                                    wob[par][:, dq * 512:(dq + 1) * 512],
                                    start=(sc == 0), stop=(sc == SC - 1)))
                vsc = pe.inc()
                wob_free[par] = (pe, vsc)
            vpe = pe.v
            wait("vector", pe, vpe)
            wait("vector", ve, v_xb)
            for dq in range(4):
                em("vector", lambda E, tsub=tsub, dq=dq, grp=grp: E.tensor_tensor(
                    xn_tm[:, tsub * 2048 + dq * 512: tsub * 2048 + (dq + 1) * 512],
                    pb[grp + dq][:, :],
                    x_tm[:, tsub * 2048 + dq * 512: tsub * 2048 + (dq + 1) * 512],
                    OP.add).then_inc(ve.h, 1))
                bank_freed(grp + dq, ve, ve.inc())
        em("vector", lambda E: E.engine_nop().then_inc(ve.h, 1))
        v_y = ve.inc()
        if k == 0:
            dbg("y0", xn_tm, ve, v_y)
        v_xfree2 = v_y
        wait("scalar", ve, v_y)
        for tsub in range(NSUB):
            vsq = emit_sq_stats(tsub, xn_tm[:, tsub * 2048:(tsub + 1) * 2048],
                                with_mean=True)
        wait("vector", se, vsq)
        for t in range(NSUB):
            em("vector", lambda E, t=t: E.tensor_tensor(
                STCOL(t, 5), STCOL(t, 14), STCOL(t, 15), OP.add))
        for t in range(NSUB):
            em("vector", lambda E, t=t: E.tensor_scalar_mul(STCOL(t, 6), STCOL(t, 5), R_D))
        for t in range(NSUB):
            em("vector", lambda E, t=t: E.tensor_tensor(
                STCOL(t, 11), STCOL(t, 0), STCOL(t, 1), OP.add))
        for t in range(NSUB):
            em("vector", lambda E, t=t: E.tensor_scalar_mul(STCOL(t, 7), STCOL(t, 11), R_D))
        for t in range(NSUB):
            em("vector", lambda E, t=t: E.tensor_mul(STCOL(t, 8), STCOL(t, 6), STCOL(t, 6)))
        for t in range(NSUB):
            em("vector", lambda E, t=t: E.tensor_tensor(
                STCOL(t, 9), STCOL(t, 7), STCOL(t, 8), OP.subtract))
        for t in range(NSUB):
            em("vector", lambda E, t=t: E.tensor_scalar_add(STCOL(t, 9), STCOL(t, 9), EPS))
        emit_rsqrt_batch(list(range(NSUB)), 9, 10)
        for t in range(NSUB):
            em("vector", lambda E, t=t: E.tensor_scalar(
                xn_tm[:, t * 2048:(t + 1) * 2048], xn_tm[:, t * 2048:(t + 1) * 2048],
                STCOL(t, 6), STCOL(t, 10), OP.subtract, OP.mult))
        em("vector", lambda E: E.engine_nop().then_inc(ve.h, 1))
        v_x2 = ve.inc()
        if k == 0:
            dbg("x2_0", xn_tm, ve, v_x2)
        wait("tensor", ve, v_x2)
        for dc in range(DC):
            bk = 6 + dc % 2
            pe_take_bank(bk)
            for i in range(NSUB):
                if i == NSUB - 1:
                    em("tensor", lambda E, dc=dc, i=i, bk=bk: E.matmul(
                        pb[bk][:, i * 128:(i + 1) * 128],
                        xn_tm[:, i * 2048 + dc * 128: i * 2048 + (dc + 1) * 128],
                        s_ident, start=(i == 0), stop=True,
                        is_transpose=True).then_inc(pe.h, 1))
                else:
                    em("tensor", lambda E, dc=dc, i=i, bk=bk: E.matmul(
                        pb[bk][:, i * 128:(i + 1) * 128],
                        xn_tm[:, i * 2048 + dc * 128: i * 2048 + (dc + 1) * 128],
                        s_ident, start=(i == 0), stop=False, is_transpose=True))
            vpe = pe.inc()
            wait("vector", pe, vpe)
            em("vector", lambda E, dc=dc, bk=bk: E.tensor_copy(
                x2_fm[:, dc * 512:(dc + 1) * 512], pb[bk][:, :]).then_inc(ve.h, 1))
            bank_freed(bk, ve, ve.inc())
        v_x2fm = ve.v
        for ffs in range(2):
            wait("vector", pe, v_hid_free)
            for hc in range(HC):
                par = hc % 2
                s, v = w13_free[par]
                wait("sync", s, v)
                em("sync", lambda E, hc=hc, par=par: E.dma_start(
                    out=w13b[par].rearrange("p (w n) -> p w n", w=2),
                    in_=w13_h[hc].rearrange("w p c q -> p w (c q)")).then_inc(dW.h, 16))
                vw = dW.inc(16)
                bu = (hc % 2) * 2
                bl = bu + 1
                pe_take_bank(bu)
                pe_take_bank(bl)
                wait("tensor", dW, vw)
                if hc == 0 and ffs == 0:
                    wait("tensor", ve, v_x2fm)
                for dc in range(DC):
                    if dc == DC - 1:
                        em("tensor", lambda E, par=par, dc=dc, bu=bu, ffs=ffs: E.matmul(
                            pb[bu][:, 0:256], w13b[par][:, dc * 128:(dc + 1) * 128],
                            x2_fm[:, dc * 512 + ffs * 256: dc * 512 + ffs * 256 + 256],
                            start=False, stop=True).then_inc(pe.h, 1))
                    else:
                        em("tensor", lambda E, par=par, dc=dc, bu=bu, ffs=ffs: E.matmul(
                            pb[bu][:, 0:256], w13b[par][:, dc * 128:(dc + 1) * 128],
                            x2_fm[:, dc * 512 + ffs * 256: dc * 512 + ffs * 256 + 256],
                            start=(dc == 0), stop=False))
                vpeU = pe.inc()
                for dc in range(DC):
                    if dc == DC - 1:
                        em("tensor", lambda E, par=par, dc=dc, bl=bl, ffs=ffs: E.matmul(
                            pb[bl][:, 0:256], w13b[par][:, 2048 + dc * 128: 2048 + (dc + 1) * 128],
                            x2_fm[:, dc * 512 + ffs * 256: dc * 512 + ffs * 256 + 256],
                            start=False, stop=True).then_inc(pe.h, 1))
                    else:
                        em("tensor", lambda E, par=par, dc=dc, bl=bl, ffs=ffs: E.matmul(
                            pb[bl][:, 0:256], w13b[par][:, 2048 + dc * 128: 2048 + (dc + 1) * 128],
                            x2_fm[:, dc * 512 + ffs * 256: dc * 512 + ffs * 256 + 256],
                            start=(dc == 0), stop=False))
                vpeL = pe.inc()
                w13_free[par] = (pe, vpeL)
                wait("scalar", pe, vpeU)
                wait("scalar", ve, su2_free[par])
                em("scalar", lambda E, hc=hc, bu=bu, par=par: E.activation(
                    su_tmp[:, par * 512:par * 512 + 256], pb[bu][:, 0:256],
                    AF.Silu).then_inc(se.h, 1))
                vse = se.inc()
                bank_freed(bu, se, vse)
                wait("vector", pe, vpeL)
                wait("vector", se, vse)
                em("vector", lambda E, hc=hc, bl=bl, par=par: E.tensor_tensor(
                    hidden[:, hc * 256:(hc + 1) * 256], pb[bl][:, 0:256],
                    su_tmp[:, par * 512:par * 512 + 256], OP.mult).then_inc(ve.h, 1))
                vv = ve.inc()
                bank_freed(bl, ve, vv)
                su2_free[par] = vv
            v_hidden = ve.v
            if k == 0 and ffs == 0:
                dbg("hidden0", hidden, ve, v_hidden)
            if k == 0 and ffs == 1:
                dbg("hidden1", hidden, ve, v_hidden)
            v_w13_done = pe.v
            wait("tensor", ve, v_hidden)
            if k == 0 and ffs == 0:
                wait("sync", dS, v_w2pp_done)
            for i in range(8):
                pe_take_bank(i)
            for hc in range(HC):
                par = hc % 2
                if hc < 2:
                    wait("sync", pe, v_w13_done)
                else:
                    s, v = w2b_free[par]
                    wait("sync", s, v)
                em("sync", lambda E, hc=hc, par=par: E.dma_start(
                    out=w2b[par].rearrange("p (q n) -> p q n", q=4),
                    in_=w2pp[hc].rearrange("q p n -> p q n")).then_inc(dW.h, 16))
                vw = dW.inc(16)
                wait("tensor", dW, vw)
                for ts2 in range(2):
                    for dq in range(4):
                        if ts2 == 1 and dq == 3:
                            em("tensor", lambda E, hc=hc, par=par, ts2=ts2, dq=dq: E.matmul(
                                pb[ts2 * 4 + dq][:, :],
                                hidden[:, hc * 256 + ts2 * 128: hc * 256 + ts2 * 128 + 128],
                                w2b[par][:, dq * 512:(dq + 1) * 512],
                                start=(hc == 0), stop=(hc == HC - 1)).then_inc(pe.h, 1))
                        else:
                            em("tensor", lambda E, hc=hc, par=par, ts2=ts2, dq=dq: E.matmul(
                                pb[ts2 * 4 + dq][:, :],
                                hidden[:, hc * 256 + ts2 * 128: hc * 256 + ts2 * 128 + 128],
                                w2b[par][:, dq * 512:(dq + 1) * 512],
                                start=(hc == 0), stop=(hc == HC - 1)))
                vhc = pe.inc()
                w2b_free[par] = (pe, vhc)
            vpe = pe.v
            v_hid_free = vpe
            w13_free = [(pe, vpe), (pe, vpe)]
            wait("vector", pe, vpe)
            for ts2 in range(2):
                tsub = ffs * 2 + ts2
                y2s = y2 if ts2 == 0 else y2b
                wait("vector", dO, v_y2_free)
                for dq in range(4):
                    em("vector", lambda E, ts2=ts2, dq=dq, tsub=tsub, y=y2s: E.tensor_tensor(
                        y[:, dq * 512:(dq + 1) * 512], pb[ts2 * 4 + dq][:, :],
                        xn_tm[:, tsub * 2048 + dq * 512: tsub * 2048 + (dq + 1) * 512],
                        OP.add).then_inc(ve.h, 1))
                    bank_freed(ts2 * 4 + dq, ve, ve.inc())
                em("vector", lambda E: E.engine_nop().then_inc(ve.h, 1))
                vv0 = ve.inc()
                if k == 0 and ffs == 0 and ts2 == 0:
                    dbg("y2pre", y2s, ve, vv0)
                if k == 0 and ffs == 1 and ts2 == 1:
                    dbgnb("y2pre11", y2s, ve, vv0)
                wait("scalar", ve, vv0)
                vsq = emit_sq_stats(tsub, y2s, with_mean=True)
                wait("vector", se, vsq)
                em("vector", lambda E, t=tsub: E.tensor_tensor(
                    STCOL(t, 5), STCOL(t, 14), STCOL(t, 15), OP.add))
                vsep()
                em("vector", lambda E, t=tsub: E.tensor_scalar_mul(STCOL(t, 6), STCOL(t, 5), R_D))
                em("vector", lambda E, t=tsub: E.tensor_tensor(
                    STCOL(t, 11), STCOL(t, 0), STCOL(t, 1), OP.add))
                vsep()
                em("vector", lambda E, t=tsub: E.tensor_scalar_mul(STCOL(t, 7), STCOL(t, 11), R_D))
                em("vector", lambda E, t=tsub: E.tensor_mul(STCOL(t, 8), STCOL(t, 6), STCOL(t, 6)))
                vsep()
                em("vector", lambda E, t=tsub: E.tensor_tensor(
                    STCOL(t, 9), STCOL(t, 7), STCOL(t, 8), OP.subtract))
                vsep()
                em("vector", lambda E, t=tsub: E.tensor_scalar_add(STCOL(t, 9), STCOL(t, 9), EPS))
                emit_rsqrt_batch([tsub], 9, 10)
                if k == 0 and ffs == 1 and ts2 == 1 and debug:
                    em("vector", lambda E: E.engine_nop().then_inc(ve.h, 1))
                    _vpa = ve.inc()
                    dbgnb("y2preapply", y2s, ve, _vpa)
                em("vector", lambda E, t=tsub, y=y2s: E.tensor_scalar(
                    y, y, STCOL(t, 6), STCOL(t, 10), OP.subtract, OP.mult).then_inc(ve.h, 1))
                vfin = ve.inc()
                wait("gpsimd", ve, vfin)
                em("gpsimd", lambda E, k=k, tsub=tsub, y=y2s: E.dma_start(
                    out=y_out[k * TT + tsub * 128: k * TT + (tsub + 1) * 128, :],
                    in_=y).then_inc(dO.h, 16))
                v_y2_free = dO.inc(16)
        v_fm2free = pe.v
        v_dO_tile = dO.v
        wob_free = [(pe, v_fm2free), (pe, v_fm2free)]

    wait("gpsimd", dO, dO.v)

    with nc.Block() as block:
        @block.sync
        def _(E):
            for fn in EMIT["sync"]:
                fn(E)

        @block.tensor
        def _(E):
            for fn in EMIT["tensor"]:
                fn(E)

        @block.vector
        def _(E):
            for fn in EMIT["vector"]:
                fn(E)

        @block.scalar
        def _(E):
            for fn in EMIT["scalar"]:
                fn(E)

        @block.gpsimd
        def _(E):
            for fn in EMIT["gpsimd"]:
                fn(E)

    st.close()
    return nc


su_free = [0, 0]
su2_free = [0, 0]


def _host_layouts(inputs):
    f32 = np.float32

    def proj_tiles(W):
        return np.ascontiguousarray(
            np.asarray(W).reshape(SC, 128, DC, 128).transpose(0, 3, 2, 1).astype(f32))

    wf = proj_tiles(inputs["L0_attn_Wf_w"])
    wzg = proj_tiles(inputs["L0_attn_Wzg_w"])
    wz = proj_tiles(inputs["L0_attn_Wz_w"])
    wga = proj_tiles(inputs["L0_attn_Wga_w"])
    wgb = proj_tiles(inputs["L0_attn_Wgb_w"])
    Wout = np.asarray(inputs["L0_attn_Wout_w"])
    wout = np.ascontiguousarray(
        Wout.reshape(4, 512, SC, 128).transpose(2, 0, 3, 1).astype(f32))
    W1 = np.asarray(inputs["L0_ffn_W1"])
    W3 = np.asarray(inputs["L0_ffn_W3"])
    w1 = W1.reshape(HC, 128, DC, 128).transpose(0, 3, 2, 1)
    w3 = W3.reshape(HC, 128, DC, 128).transpose(0, 3, 2, 1)
    w13 = np.ascontiguousarray(np.stack([w1, w3], axis=1).astype(f32))
    W2T = np.ascontiguousarray(np.asarray(inputs["L0_ffn_W2"]).T)
    w2sl = np.ascontiguousarray(
        W2T.reshape(HC, 128, NSL, SLW).transpose(2, 0, 1, 3).astype(f32))
    wfb = np.ascontiguousarray(
        np.asarray(inputs["L0_attn_Wf_b"]).reshape(SC, 128).T.astype(f32))
    rmsw = np.asarray(inputs["L0_attn_rms_w"]).reshape(1, D).astype(f32)
    woutb = np.asarray(inputs["L0_attn_Wout_b"]).reshape(1, D).astype(f32)
    ident = np.eye(128, dtype=f32)
    idx = np.arange(128)
    pc = idx[:, None] & idx[None, :]
    pop = np.zeros_like(pc)
    v = pc.copy()
    while v.any():
        pop += v & 1
        v >>= 1
    hadp = (((-1.0) ** pop) / np.sqrt(float(H))).astype(f32)
    return dict(wf=wf, wzg=wzg, wz=wz, wga=wga, wgb=wgb, wout=wout, w13=w13,
                w2sl=w2sl, wfb=wfb, rmsw=rmsw, woutb=woutb, ident=ident, hadp=hadp)


def kernel(**inputs):
    global su_free, su2_free
    su_free = [0, 0]
    su2_free = [0, 0]
    x = np.asarray(inputs["L0_attn_x"], dtype=np.float32)
    shared = _host_layouts(inputs)
    nc = build_nc()
    in_maps = []
    for c in range(8):
        b, hf = c // 2, c % 2
        mask = np.zeros(8, np.float32)
        if hf == 1:
            mask[c - 1] = 1.0
        m = dict(shared)
        m["x"] = np.ascontiguousarray(x[b, hf * TOK:(hf + 1) * TOK, :])
        m["maskbc"] = np.ascontiguousarray(np.tile(mask[None, :], (128, 1)))
        in_maps.append(m)
    res = run_bass_kernel_spmd(nc, in_maps, core_ids=list(range(8)))
    out = np.empty((B, T, D), np.float32)
    for c in range(8):
        b, hf = c // 2, c % 2
        out[b, hf * TOK:(hf + 1) * TOK, :] = res.results[c]["y"]
    return out
